# version 20
# baseline (speedup 1.0000x reference)
import numpy as np
import ml_dtypes
from contextlib import ExitStack

import concourse.bass as bass
import concourse.mybir as mybir
from concourse.bass_utils import run_bass_kernel_spmd

F32 = mybir.dt.float32
BF16 = mybir.dt.bfloat16
AF = mybir.ActivationFunctionType
ALU = mybir.AluOpType

ENGINES = ("sync", "tensor", "vector", "scalar", "gpsimd")


def default_cfg():
    return dict(B=2, S=2048, D=1024, DI=2048, DS=16, H=16, PH=64, AH=64, TC=512, TS=512)


class Ctr:
    __slots__ = ("name", "n", "producer", "handle")

    def __init__(self, name, producer):
        self.name, self.n, self.producer, self.handle = name, 0, producer, None


class Prog:
    def __init__(self, nc, stack):
        self.nc, self.stack = nc, stack
        self.ops = {e: [] for e in ENGINES}
        self.ctrs = []

    def ctr(self, name, producer):
        c = Ctr(name, producer)
        c.handle = self.stack.enter_context(self.nc.semaphore(name))
        self.ctrs.append(c)
        return c

    def op(self, eng, fn, waits=(), inc=None, inc_n=1):
        wl = []
        for w in waits:
            if w is None:
                continue
            c, v = w if isinstance(w, tuple) else (w, w.n)
            if v <= 0:
                continue
            if c.producer == eng:
                continue
            wl.append((c, v))
        if inc is not None:
            assert inc.producer == eng or (inc.producer == "dma" and eng in ("sync", "gpsimd", "scalar", "vector")), \
                (inc.name, inc.producer, eng)
            inc.n += inc_n
        self.ops[eng].append((wl, fn, inc, inc_n))

    def emit(self, block):
        def mk(name):
            def section(eng):
                waited = {}
                for wl, fn, inc, inc_n in self.ops[name]:
                    for c, v in wl:
                        if waited.get(id(c), 0) >= v:
                            continue
                        eng.wait_ge(c.handle, v)
                        waited[id(c)] = v
                    instr = fn(eng)
                    if inc is not None:
                        instr.then_inc(inc.handle, inc_n)
            return section
        block.sync(mk("sync"))
        block.tensor(mk("tensor"))
        block.vector(mk("vector"))
        block.scalar(mk("scalar"))
        block.gpsimd(mk("gpsimd"))


def host_inputs(cfg, shard):
    c = cfg
    B, S, D, DI, DS = c["B"], c["S"], c["D"], c["DI"], c["DS"]
    H, PH, AH = c["H"], c["PH"], c["AH"]
    ND, NI = D // 128, DI // 128
    HA, HP = H * AH, H * PH
    NPG, NPGA = HP // 128, HA // 128
    HPG = 128 // PH

    def pm(a, last):
        n = a.shape[0] // 128
        return np.ascontiguousarray(a.reshape(n, 128, last).transpose(1, 0, 2))

    f = np.float32
    bf = ml_dtypes.bfloat16
    o = {}
    o["x"] = np.ascontiguousarray(shard["L0_attn_a_x"].transpose(0, 2, 1)).astype(f)
    o["winT"] = pm(np.ascontiguousarray(shard["L0_attn_a_W_in"].T).astype(f), 2 * DI)
    o["woutT"] = pm(np.ascontiguousarray(shard["L0_attn_a_W_out"].T).astype(f), D)
    wx = shard["L0_attn_a_W_x"].astype(f)
    wxp = np.zeros((DI, 80), f)
    wxp[:, 0] = wx[0]
    wxp[:, 32:32 + DS] = wx[1:1 + DS].T
    wxp[:, 64:64 + DS] = wx[1 + DS:].T
    o["wxT"] = pm(wxp, 80)
    o["alog"] = pm(shard["L0_attn_a_A_log"].astype(f), DS)
    cw = shard["L0_attn_a_conv_w"][:, 0, :].astype(f)
    cdg = np.zeros((128, 4, NI * 128), f)
    for i in range(NI):
        blk = cw[i * 128:(i + 1) * 128]
        for j in range(4):
            cdg[np.arange(128), j, i * 128 + np.arange(128)] = blk[:, j]
    o["cdg"] = cdg
    o["cb"] = pm(shard["L0_attn_a_conv_b"].astype(f)[:, None], 1)[:, :, 0]
    o["wrT"] = pm(np.ascontiguousarray(shard["L0_ffn_W_r"].T).astype(f), D)
    o["wkT"] = pm(np.ascontiguousarray(shard["L0_ffn_W_k"].T).astype(f), D)
    o["wvT"] = pm(np.ascontiguousarray(shard["L0_ffn_W_v"].T).astype(f), D)
    o["mur"] = pm(shard["L0_ffn_mu_r"].astype(f)[:, None], 1)[:, :, 0]
    o["muk"] = pm(shard["L0_ffn_mu_k"].astype(f)[:, None], 1)[:, :, 0]
    k123 = np.zeros((D, 80), f)
    k123[:, 0:H] = shard["L0_attn_b_k1"].T
    k123[:, 32:32 + H] = shard["L0_attn_b_k2"].T
    k123[:, 64:64 + H] = shard["L0_attn_b_k3"].T
    o["k123T"] = pm(k123, 80)
    o["vwT"] = pm(np.ascontiguousarray(
        shard["L0_attn_b_value_weight"].transpose(1, 0, 2).reshape(D, HA)).astype(f), HA)
    o["owT"] = pm(np.ascontiguousarray(
        shard["L0_attn_b_output_weight"].transpose(0, 2, 1).reshape(HA, D)).astype(f), D)
    for nm, key in (("a1", "L0_attn_b_a1"), ("b1", "L0_attn_b_b1"),
                    ("a2", "L0_attn_b_a2"), ("b2", "L0_attn_b_b2")):
        o[nm] = np.ascontiguousarray(shard[key].reshape(HP).reshape(NPG, 128).T).astype(f)
    cflat = shard["L0_attn_b_c"].reshape(H, PH).astype(f)
    cwsel = np.zeros((128, NPG, H), f)
    for pg in range(NPG):
        for j in range(HPG):
            h = pg * HPG + j
            cwsel[j * PH:(j + 1) * PH, pg, h] = cflat[h]
    o["cwsel"] = cwsel
    o["i128"] = np.eye(128, dtype=bf)
    ipad = np.zeros((H, 80), bf)
    for h in range(H):
        ipad[h, h] = 1
    o["i16"] = ipad
    o["ones1x128f"] = np.ones((1, 128), f)
    o["onesbig"] = np.ones((128, 1024), dtype=bf)
    oh = np.zeros((DS, DS, 128), bf)
    for s in range(DS):
        oh[s, s, :] = 1
    o["onehot"] = oh
    HPA = 128 // AH
    bsel = np.zeros((H, NPGA, 128), bf)
    for pg in range(NPGA):
        for p in range(128):
            bsel[pg * HPA + p // AH, pg, p] = 1
    o["bcsel"] = bsel
    o["nrow"] = np.linspace(0.0, 1.0, S, dtype=f)[None, :]
    o["konst"] = np.tile(np.array([[1e-5, 1.0, -0.5]], f), (128, 1))
    return o


def build_nc(cfg):
    nc = bass.Bass("TRN2", target_bir_lowering=False, debug=False,
                   detect_race_conditions=False)
    c = cfg
    B, S, D, DI, DS = c["B"], c["S"], c["D"], c["DI"], c["DS"]
    H, PH, AH, TC, TS = c["H"], c["PH"], c["AH"], c["TC"], c["TS"]
    ND, NI = D // 128, DI // 128
    HA, HP = H * AH, H * PH
    NPG, NPGA = HP // 128, HA // 128
    HPG = 128 // PH
    NCH = S // TC
    NSL = S // TS
    HALF = S // 2
    NHS = HALF // TS
    assert TC == TS, "chunk width = tile width assumed"

    def din(name, shape, dtype=F32):
        return nc.dram_tensor(name, list(shape), dtype, kind="ExternalInput").ap()

    x_d = din("x", [B, D, S])
    winT_d = din("winT", [128, ND, 2 * DI])
    woutT_d = din("woutT", [128, NI, D])
    wxT_d = din("wxT", [128, NI, 80])
    alog_d = din("alog", [128, NI, DS])
    cdg_d = din("cdg", [128, 4, NI * 128])
    cb_d = din("cb", [128, NI])
    wrT_d = din("wrT", [128, ND, D])
    wkT_d = din("wkT", [128, ND, D])
    wvT_d = din("wvT", [128, ND, D])
    mur_d = din("mur", [128, ND])
    muk_d = din("muk", [128, ND])
    k123T_d = din("k123T", [128, ND, 80])
    vwT_d = din("vwT", [128, ND, HA])
    owT_d = din("owT", [128, NPGA, D])
    a1_d, b1_d = din("a1", [128, NPG]), din("b1", [128, NPG])
    a2_d, b2_d = din("a2", [128, NPG]), din("b2", [128, NPG])
    cwsel_d = din("cwsel", [128, NPG, H])
    i128_d = din("i128", [128, 128], BF16)
    i16_d = din("i16", [H, 80], BF16)
    ones1_d = din("ones1x128f", [1, 128])
    onesbig_d = din("onesbig", [128, 1024], BF16)
    onehot_d = din("onehot", [DS, DS, 128], BF16)
    bcsel_d = din("bcsel", [H, NPGA, 128], BF16)
    nrow_d = din("nrow", [1, S])
    konst_d = din("konst", [128, 3])
    out_d = nc.dram_tensor("out", [B, D, S], F32, kind="ExternalOutput").ap()

    def dscr(name, nslab, ksub):
        return nc.dram_tensor(name, [nslab, 128, ksub * 128], BF16, kind="Internal").ap()
    winb_d = dscr("winb", 2 * DI // 128, ND)
    woutb_d = dscr("woutb", 2 * ND, NI // 2)
    cdgb_d = dscr("cdgb", NI, 4)
    wrb_d = dscr("wrb", ND, ND)
    wkb_d = dscr("wkb", ND, ND)
    wvb_d = dscr("wvb", ND, ND)
    vwb_d = dscr("vwb", NPGA, ND)
    owb_d = dscr("owb", ND, NPGA)

    with ExitStack() as stack:
        sb = lambda name, shape, dt=F32: stack.enter_context(nc.sbuf_tensor(name, list(shape), dt))
        P = Prog(nc, stack)

        resid = sb("resid", [128, ND, B, S], BF16)
        A_sb = sb("A_sb", [128, NI, DS])
        wxT_sb = sb("wxT_sb", [128, NI, 80], BF16)
        cb_sb = sb("cb_sb", [128, NI])
        mur_sb = sb("mur_sb", [128, ND])
        muk_sb = sb("muk_sb", [128, ND])
        k123_sb = sb("k123_sb", [128, ND, 80], BF16)
        a1_sb, b1_sb = sb("a1_sb", [128, NPG]), sb("b1_sb", [128, NPG])
        a2_sb, b2_sb = sb("a2_sb", [128, NPG]), sb("b2_sb", [128, NPG])
        cwsel_sb = sb("cwsel_sb", [128, NPG, H], BF16)
        i128_sb = sb("i128_sb", [128, 128], BF16)
        i16_sb = sb("i16_sb", [H, 80], BF16)
        ones1_sb = sb("ones1_sb", [1, 128])
        onesbig_sb = sb("onesbig_sb", [128, 1024], BF16)
        onehot_sb = sb("onehot_sb", [DS, DS, 128], BF16)
        bcsel_sb = sb("bcsel_sb", [H, NPGA, 128], BF16)
        konst_sb = sb("konst_sb", [128, 3])
        barsc = sb("barsc", [1, 8])
        hcarry = sb("hcarry", [128, NI * DS])
        xtail = sb("xtail", [128, NI, 3], BF16)
        ncarry = sb("ncarry", [128, NPGA])
        dcarry = sb("dcarry", [16, 1])
        brow = sb("brow", [16, TS], BF16)
        crow = sb("crow", [16, TS], BF16)
        lrow = [sb(f"lrow{i}", [1, TS]) for i in range(3)]
        LNBC = [None, None]
        lrow.append(lrow[1])
        slab = [sb(f"slab{i}", [128, 2048], BF16) for i in range(2)]

        ARB_N = 48672
        ARF_N = 4352
        arB = sb("arB", [128, ARB_N], BF16)
        arF = sb("arF", [128, ARF_N])

        class Arena:
            def __init__(self, t, size):
                self.t, self.size, self.off = t, size, 0
            def view(self, *shape):
                n = int(np.prod(shape))
                assert self.off + n <= self.size, (self.off, n, self.size)
                v = self.t[:, self.off:self.off + n]
                self.off += n
                if len(shape) > 1:
                    kw = {chr(ord('a') + i): shape[i] for i in range(len(shape))}
                    pat = " ".join(chr(ord('a') + i) for i in range(len(shape)))
                    v = v.rearrange(f"p ({pat}) -> p {pat}", **kw)
                return v
            def reset(self):
                self.off = 0

        AB, AFr = Arena(arB, ARB_N), Arena(arF, ARF_N)

        psa = lambda name, shape: stack.enter_context(nc.psum_tensor(name, list(shape), F32))
        PS = [psa(f"ps{i}", [128, TS]) for i in range(6)]
        PSm = psa("psm", [80, TS])
        PSb = psa("psb", [128, TS])

        cd_w = P.ctr("cd_w", "dma")
        cd_x = P.ctr("cd_x", "dma")
        cd_slab = [P.ctr(f"cd_slab{i}", "dma") for i in range(2)]
        cd_st = P.ctr("cd_st", "dma")
        cd_out = P.ctr("cd_out", "dma")
        cv = P.ctr("cv", "vector")
        cs = P.ctr("cs", "scalar")
        ct = P.ctr("ct", "tensor")
        cg = P.ctr("cg", "gpsimd")

        M = {}

        def mset(name, ctr):
            M[name] = (ctr, ctr.n)

        def mk(name, default=None):
            return M.get(name, default)

        def barrier():
            P.op("vector", lambda en: en.engine_nop(), inc=cv)
            P.op("gpsimd", lambda en: en.engine_nop(), inc=cg)
            P.op("scalar", lambda en: en.copy(barsc[:, 0:4], barsc[:, 4:8]),
                 waits=[(cv, 1)], inc=cs)
            P.op("tensor", lambda en: en.matmul(
                PSb[0:1, 0:8], onesbig_sb[0:1, 0:1], onesbig_sb[0:1, 0:8],
                start=True, stop=True), waits=[mk("psb_free"), mk("consts"), (cd_w, 16)], inc=ct)
            mset("psb_free", ct)
            marks = [(ct, ct.n), (cv, cv.n), (cs, cs.n), (cg, cg.n)]
            for e in ("tensor", "vector", "scalar", "gpsimd", "sync"):
                P.op(e, lambda en: en.nop(), waits=marks)

        def dma(dst, src, inc, waits=()):
            P.op("sync", lambda e, d=dst, s=src: e.dma_start(out=d, in_=s),
                 waits=waits, inc=inc, inc_n=16)

        for dst, src in [
            (cb_sb[:], cb_d[:]), (mur_sb[:], mur_d[:]), (muk_sb[:], muk_d[:]),
            (a1_sb[:], a1_d[:]), (b1_sb[:], b1_d[:]), (a2_sb[:], a2_d[:]),
            (b2_sb[:], b2_d[:]), (i128_sb[:], i128_d[:]), (i16_sb[:], i16_d[:]),
            (ones1_sb[:], ones1_d[:]), (onesbig_sb[:], onesbig_d[:]),
            (onehot_sb[:], onehot_d[:]), (bcsel_sb[:], bcsel_d[:]),
            (konst_sb[:], konst_d[:]),
        ]:
            dma(dst, src, cd_w)

        P.op("vector", lambda e: e.memset(barsc[:], 0.0), inc=cv)
        stg = [AFr.view(2048) for _ in range(2)]

        def stg3(i, a, b):
            return stg[i][:, 0:a * b].rearrange("p (a b) -> p a b", a=a, b=b)

        dma(stg3(0, NI, DS), alog_d[:], cd_w)
        P.op("scalar", lambda e: e.activation(A_sb[:], stg3(0, NI, DS), AF.Exp),
             waits=[cd_w], inc=cs)
        P.op("vector", lambda e: e.tensor_scalar_mul(A_sb[:], A_sb[:], -1.0),
             waits=[cs], inc=cv)
        mset("A", cv)
        dma(stg3(1, NI, 80), wxT_d[:], cd_w)
        P.op("vector", lambda e: e.tensor_copy(wxT_sb[:], stg3(1, NI, 80)),
             waits=[cd_w], inc=cv)
        dma(stg3(0, ND, 80), k123T_d[:], cd_w, waits=[mk("A")])
        P.op("vector", lambda e: e.tensor_copy(k123_sb[:], stg3(0, ND, 80)),
             waits=[cd_w], inc=cv)
        dma(stg3(1, NPG, H), cwsel_d[:], cd_w, waits=[(cv, cv.n)])
        P.op("vector", lambda e: e.tensor_copy(cwsel_sb[:], stg3(1, NPG, H)),
             waits=[cd_w], inc=cv)
        mset("consts", cv)

        for b in range(B):
            for nd in range(ND):
                sl = nd % 2
                w = [(cv, cv.n)] if (b * ND + nd) >= 2 else [mk("consts")]
                dma(stg[sl][0:128, 0:S] if S <= 2048 else None,
                    x_d[b, nd * 128:(nd + 1) * 128, :], cd_x, waits=w)
                P.op("vector", lambda e, sl=sl, nd=nd, b=b: e.tensor_copy(
                    resid[:, nd, b, :], stg[sl][:, 0:S]), waits=[cd_x], inc=cv)
            mset(f"xcast{b}", cv)

        fam_marks = {}

        def pre_dma(dst, srcap):
            P.op("gpsimd", lambda e, d=dst, s=srcap: e.dma_start(out=d, in_=s),
                 inc=cd_st, inc_n=16)
            return (cd_st, cd_st.n)

        def preprocess(dram_f32, dram_b16, ksub, nslabs, fam):
            fam_marks[fam] = []
            for j in range(nslabs):
                m = pre_dma(dram_b16[j], dram_f32[:, :, j * 128:(j + 1) * 128])
                fam_marks[fam].append(m)

        preprocess(winT_d, winb_d, ND, 2 * DI // 128, "win")
        preprocess(cdg_d, cdgb_d, 4, NI, "cdg")
        fam_marks["wout"] = []
        for ec in range(ND):
            for hhf in range(2):
                m = pre_dma(woutb_d[2 * ec + hhf],
                            woutT_d[:, hhf * (NI // 2):(hhf + 1) * (NI // 2),
                                    ec * 128:(ec + 1) * 128])
                fam_marks["wout"].append(m)
        preprocess(wrT_d, wrb_d, ND, ND, "wr")
        preprocess(wkT_d, wkb_d, ND, ND, "wk")
        preprocess(wvT_d, wvb_d, ND, ND, "wv")
        preprocess(vwT_d, vwb_d, ND, NPGA, "vw")
        preprocess(owT_d, owb_d, NPGA, ND, "ow")
        fam_of = {id(winb_d): "win", id(cdgb_d): "cdg", id(woutb_d): "wout",
                  id(wrb_d): "wr", id(wkb_d): "wk", id(wvb_d): "wv",
                  id(vwb_d): "vw", id(owb_d): "ow"}
        P.op("vector", lambda e: e.nop(), waits=[(cd_w, cd_w.n)], inc=cv)
        barrier()

        slab_free = [None, None]
        slab_seq = [0]

        def load_slab(fam, j, ksub, cnt=1):
            slot = slab_seq[0] % 2
            slab_seq[0] += 1
            marks = fam_marks[fam_of[id(fam)]]
            w = [slab_free[slot]] + [marks[j + k] for k in range(cnt)]
            if cnt == 1:
                srcap = fam[j]
            else:
                srcap = fam[j:j + cnt].rearrange("j p k -> p j k")
            dma(slab[slot][:, 0:cnt * ksub * 128], srcap, cd_slab[slot], waits=w)
            return slot, (cd_slab[slot], cd_slab[slot].n)

        def slab_stat(slot, ksub, kc, sub=0):
            return slab[slot][:, sub * ksub * 128:(sub + 1) * ksub * 128].rearrange(
                "p (a b) -> p a b", a=ksub, b=128)[:, kc, :]

        def slab_done(slot):
            slab_free[slot] = (ct, ct.n)

        def emit_ln(x_ap, xh_ap, sq_ap, x_waits, xh_free):
            if not isinstance(x_waits, list):
                x_waits = [x_waits]
            P.op("scalar", lambda e: e.activation(sq_ap, x_ap, AF.Square),
                 waits=x_waits + [mk("sq_free"), xh_free], inc=cs)
            sq_done = (cs, cs.n)
            P.op("tensor", lambda e: e.matmul(
                PSm[0:1, :], onesbig_sb[:, 0:1], x_ap[:, 0, :], start=True, stop=(ND == 1)),
                 waits=x_waits + [mk("psm_free")])
            for nd in range(1, ND):
                P.op("tensor", lambda e, nd=nd: e.matmul(
                    PSm[0:1, :], onesbig_sb[:, 0:1], x_ap[:, nd, :],
                    start=False, stop=(nd == ND - 1)))
            P.op("tensor", lambda e: e.matmul(
                PSm[32:33, :], onesbig_sb[:, 0:1], sq_ap[:, 0, :], start=True, stop=(ND == 1)),
                 waits=[sq_done])
            for nd in range(1, ND):
                P.op("tensor", lambda e, nd=nd: e.matmul(
                    PSm[32:33, :], onesbig_sb[:, 0:1], sq_ap[:, nd, :],
                    start=False, stop=(nd == ND - 1)), inc=(ct if nd == ND - 1 else None))
            if ND == 1:
                P.op("tensor", lambda e: e.matmul(
                    PSm[32:33, :], onesbig_sb[:, 0:1], sq_ap[:, 0, :],
                    start=True, stop=True), inc=ct) if False else None
            mset("sq_free", ct)
            stats = (ct, ct.n)
            P.op("vector", lambda e: e.tensor_scalar_mul(lrow[0][:], PSm[0:1, :], 1.0 / D),
                 waits=[stats, mk("lrow_free")])
            P.op("vector", lambda e: e.tensor_scalar_mul(lrow[1][:], PSm[32:33, :], 1.0 / D),
                 inc=cv)
            mset("psm_free", cv)
            P.op("vector", lambda e: e.tensor_tensor(lrow[2][:], lrow[0][:], lrow[0][:], ALU.mult))
            P.op("vector", lambda e: e.tensor_sub(lrow[1][:], lrow[1][:], lrow[2][:]), inc=cv)
            var_done = (cv, cv.n)
            P.op("scalar", lambda e: e.activation(lrow[2][:], lrow[1][:], AF.Ln, bias=konst_sb[0:1, 0:1]),
                 waits=[var_done])
            P.op("scalar", lambda e: e.activation(lrow[2][:], lrow[2][:], AF.Exp, scale=konst_sb[0:1, 2:3]),
                 inc=cs)
            inv_done = (cs, cs.n)
            P.op("vector", lambda e: e.scalar_tensor_tensor(
                lrow[3][:], lrow[0][:], -1.0, lrow[2][:], ALU.mult, ALU.mult),
                 waits=[inv_done], inc=cv)
            minv_done = (cv, cv.n)
            P.op("tensor", lambda e: e.matmul(PS[4][:, :], ones1_sb[:], lrow[2][:],
                                              start=True, stop=True),
                 waits=[inv_done, mk("bc4_free")])
            P.op("tensor", lambda e: e.matmul(PS[5][:, :], ones1_sb[:], lrow[3][:],
                                              start=True, stop=True),
                 waits=[minv_done, mk("bc5_free")], inc=ct)
            bcs = (ct, ct.n)
            mset("lrow_free", ct)
            lnbcA, lnbcB = LNBC[0]
            lnbc_wait = LNBC[1]() if callable(LNBC[1]) else LNBC[1]
            P.op("scalar", lambda e, lnbcA=lnbcA: e.copy(lnbcA, PS[4][:, :]),
                 waits=[bcs, mk("lnbc_free"), lnbc_wait])
            P.op("scalar", lambda e, lnbcB=lnbcB: e.copy(lnbcB, PS[5][:, :]), inc=cs)
            lnbc_done = (cs, cs.n)
            mset("bc4_free", cs)
            mset("bc5_free", cs)
            inv_bc = lnbcA.rearrange("p (o t) -> p o t", o=1).broadcast_to([128, ND, TS])
            minv_bc = lnbcB.rearrange("p (o t) -> p o t", o=1).broadcast_to([128, ND, TS])
            P.op("vector", lambda e: e.tensor_tensor(xh_ap, x_ap, inv_bc, ALU.mult),
                 waits=[lnbc_done, xh_free])
            P.op("vector", lambda e: e.tensor_tensor(xh_ap, xh_ap, minv_bc, ALU.add), inc=cv)
            xh_done = (cv, cv.n)
            mset("lnbc_free", cv)
            return xh_done

        ps_free = {i: None for i in range(6)}

        def attn_a():
            AB.reset(); AFr.reset()
            xh = AB.view(ND, TS)
            xp = [AB.view(TS + 3) for _ in range(2)]
            LNBC[0] = (xp[0][:, 0:TS], xp[1][:, 0:TS])
            LNBC[1] = lambda: xp_free[1]
            xc = AB.view(NI, TS)
            sz = AB.view(NI, TS)
            hreg = AB.view(DS, 1 + TS)
            hh = [AB.view(TS) for _ in range(3)]
            wmul = [AB.view(1 + TS) for _ in range(2)]
            wm_free = [None, None]
            dbbc = AB.view(DS, TS)
            cbc = AB.view(DS, TS)
            drow = AFr.view(TS)[0:1, :]
            dA = [AFr.view(1 + TS) for _ in range(4)]
            dbc_f = AFr.view(1 + TS)
            hh_free = [None, None]
            dA_free = [None] * 4
            xp_free = [None, None]
            tile = [0]
            pend_ln = [None, None]
            hc_mark = {}
            P.op("vector", lambda e: e.memset(dbc_f[:, 0:1], 0.0), inc=cv)
            for b in range(B):
                P.op("vector", lambda e: e.memset(hcarry[:], 0.0))
                P.op("vector", lambda e: e.memset(xtail[:], 0.0), inc=cv)
                carry0 = (cv, cv.n)
                for ch in range(NCH):
                    t0 = ch * TC
                    xs = resid[:, :, b, t0:t0 + TC]
                    if pend_ln[0] is not None and pend_ln[1] == (b, ch):
                        xh_done = pend_ln[0]
                        pend_ln[0] = None
                    else:
                        xh_done = emit_ln(xs, xh, xh,
                                          mk(f"xcast{b}"), mk("xh_free"))
                    a_stop = cfg.get("a_stop")
                    if a_stop == "ln":
                        P.op("vector", lambda e, b=b, t0=t0: e.tensor_copy(
                            resid[:, :, b, t0:t0 + TC], xh), waits=[xh_done], inc=cv)
                        mset("xh_free", cv) if False else None
                        continue
                    xc_marks = []
                    conv_q = []
                    cstate = {}
                    def emit_conv(job):
                        i, xps, xp_rdy = job
                        if i % 2 == 0:
                            cstate["slot"], cstate["rdy"] = load_slab(cdgb_d, i, 4, cnt=2)
                        cslot, crdy, csub = cstate["slot"], cstate["rdy"], i % 2
                        for j in range(4):
                            P.op("tensor", lambda e, cslot=cslot, j=j, xps=xps, csub=csub: e.matmul(
                                PS[2][:, :], slab_stat(cslot, 4, j, csub), xp[xps][:, j:j + TS],
                                start=(j == 0), stop=(j == 3)),
                                 waits=([crdy, xp_rdy, ps_free[2]] if j == 0 else []),
                                 inc=(ct if j == 3 else None))
                        if i % 2 == 1:
                            slab_done(cslot)
                        xp_free[xps] = (ct, ct.n)
                        P.op("scalar", lambda e, i=i: e.activation(
                            xc[:, i, :], PS[2][:, :], AF.Silu, bias=cb_sb[:, i:i + 1]),
                             waits=[(ct, ct.n)], inc=cs)
                        ps_free[2] = (cs, cs.n)
                        xc_marks.append((cs, cs.n))
                    for i in range(NI):
                        if i % 2 == 0:
                            slot, rdy = load_slab(winb_d, i, ND, cnt=2)
                        sub = i % 2
                        pb = i % 2
                        for kc in range(ND):
                            P.op("tensor", lambda e, slot=slot, kc=kc, pb=pb, sub=sub: e.matmul(
                                PS[pb][:, :], slab_stat(slot, ND, kc, sub), xh[:, kc, :],
                                start=(kc == 0), stop=(kc == ND - 1)),
                                 waits=([rdy, xh_done, ps_free[pb]] if kc == 0 else []),
                                 inc=(ct if kc == ND - 1 else None))
                        if i % 2 == 1:
                            slab_done(slot)
                        mmd = (ct, ct.n)
                        xps = i % 2
                        P.op("scalar", lambda e, xps=xps, pb=pb: e.copy(
                            xp[xps][:, 3:TS + 3], PS[pb][:, :]),
                             waits=[mmd, xp_free[xps]], inc=cs)
                        ps_free[pb] = (cs, cs.n)
                        P.op("vector", lambda e, xps=xps, i=i: e.tensor_copy(
                            xp[xps][:, 0:3], xtail[:, i, :]),
                             waits=[(cs, cs.n), carry0], inc=cv)
                        xp_rdy = (cv, cv.n)
                        P.op("vector", lambda e, xps=xps, i=i: e.tensor_copy(
                            xtail[:, i, :], xp[xps][:, TS:TS + 3]), inc=cv)
                        conv_q.append((i, xps, xp_rdy))
                        if len(conv_q) >= 2:
                            emit_conv(conv_q.pop(0))
                    while conv_q:
                        emit_conv(conv_q.pop(0))
                    for i in range(NI):
                        P.op("tensor", lambda e, i=i: e.matmul(
                            PSm[0:80, :], wxT_sb[:, i, :], xc[:, i, :],
                            start=(i == 0), stop=(i == NI - 1)),
                             waits=[xc_marks[i]] + ([mk("psm_free"), mk("rowbc_done")]
                                    if i == 0 else []),
                             inc=(ct if i == NI - 1 else None))
                    wx_done = (ct, ct.n)
                    P.op("vector", lambda e: e.tensor_scalar(
                        lrow[0][:], PSm[0:1, :], -1.0, None, ALU.mult),
                         waits=[wx_done, mk("lrow_free")])
                    P.op("vector", lambda e: e.tensor_tensor(
                        lrow[0][:], lrow[0][:], PSm[0:1, :], ALU.min), inc=cv)
                    na_done = (cv, cv.n)
                    P.op("scalar", lambda e: e.activation(lrow[1][:], lrow[0][:], AF.Exp),
                         waits=[na_done])
                    P.op("scalar", lambda e: e.activation(lrow[1][:], lrow[1][:], AF.Ln, bias=konst_sb[0:1, 1:2]),
                         inc=cs)
                    ln1p = (cs, cs.n)
                    P.op("vector", lambda e: e.tensor_scalar_max(lrow[2][:], PSm[0:1, :], 0.0),
                         waits=[wx_done])
                    P.op("vector", lambda e: e.tensor_tensor(
                        drow[:], lrow[1][:], lrow[2][:], ALU.add),
                         waits=[ln1p, mk("drow_free")], inc=cv)
                    delta_done = (cv, cv.n)
                    mset("lrow_free", cv)
                    P.op("vector", lambda e: e.tensor_copy(brow[:], PSm[32:48, :]),
                         waits=[wx_done, mk("rowbc_done")])
                    P.op("vector", lambda e: e.tensor_copy(crow[:], PSm[64:80, :]), inc=cv)
                    rows_done = (cv, cv.n)
                    mset("psm_free", cv)
                    P.op("tensor", lambda e: e.matmul(PSb[:, :], ones1_sb[:], drow[:],
                                                      start=True, stop=True),
                         waits=[delta_done, mk("psb_free")], inc=ct)
                    mset("drow_free", ct)
                    P.op("vector", lambda e: e.tensor_copy(dbc_f[:, 1:1 + TS], PSb[:, :]),
                         waits=[(ct, ct.n), mk("dbc_free"), (cs, cs.n)], inc=cv)
                    dbc_done = (cv, cv.n)
                    mset("psb_free", cv)
                    rot = [PSb, PS[4], PS[5]]
                    rot_free = [mk("psb_free"), mk("bc4_free"), mk("bc5_free")]
                    ri = [0]
                    def bc_one(stat_s, row, dst_emit):
                        k = ri[0] % 3; ri[0] += 1
                        bank = rot[k]
                        P.op("tensor", lambda e, stat_s=stat_s, row=row, bank=bank: e.matmul(
                            bank[:, :], onehot_sb[:, stat_s, :], row, start=True, stop=True),
                             waits=[rows_done, rot_free[k], mk("bcast_consumed")], inc=ct)
                        dst_emit(bank, (ct, ct.n))
                        rot_free[k] = (cv, cv.n)
                    for s in range(DS):
                        bc_one(s, brow[:], lambda bank, w, s=s: P.op(
                            "vector", lambda e, s=s, bank=bank: e.tensor_tensor(
                                dbbc[:, s, :], dbc_f[:, 1:1 + TS], bank[:, :], ALU.mult),
                            waits=[w], inc=cv))
                        bc_one(s, crow[:], lambda bank, w, s=s: P.op(
                            "vector", lambda e, s=s, bank=bank: e.tensor_copy(
                                cbc[:, s, :], bank[:, :]),
                            waits=[w], inc=cv))
                    bcast_done = (cv, cv.n)
                    mset("psb_free", cv)
                    mset("bc4_free", cv)
                    mset("bc5_free", cv)
                    mset("rowbc_done", ct)
                    z_marks = []
                    for i in range(NI):
                        if i % 2 == 0:
                            slot, rdy = load_slab(winb_d, NI + i, ND, cnt=2)
                        sub = i % 2
                        pb = i % 2
                        for kc in range(ND):
                            P.op("tensor", lambda e, slot=slot, kc=kc, pb=pb, sub=sub: e.matmul(
                                PS[pb][:, :], slab_stat(slot, ND, kc, sub), xh[:, kc, :],
                                start=(kc == 0), stop=(kc == ND - 1)),
                                 waits=([rdy, ps_free[pb], mk("ymix_prev")] if kc == 0 else []),
                                 inc=(ct if kc == ND - 1 else None))
                        if i % 2 == 1:
                            slab_done(slot)
                        P.op("scalar", lambda e, i=i, pb=pb: e.activation(
                            sz[:, i, :], PS[pb][:, :], AF.Silu),
                             waits=[(ct, ct.n)], inc=cs)
                        ps_free[pb] = (cs, cs.n)
                        z_marks.append((cs, cs.n))
                    mset("xh_free", ct)
                    if a_stop == "z":
                        P.op("vector", lambda e, b=b, t0=t0: e.tensor_copy(
                            resid[:, :, b, t0:t0 + TC], sz[:, 0:ND, :]),
                             waits=[z_marks[-1]], inc=cv)
                        continue
                    if a_stop == "wx":
                        P.op("vector", lambda e, b=b, t0=t0: e.tensor_copy(
                            resid[:, :, b, t0:t0 + TC], dbbc[:, 0:ND, :]),
                             waits=[bcast_done], inc=cv)
                        continue
                    for i in range(NI):
                        if i == NI // 2:
                            nb, nch = (b, ch + 1) if ch + 1 < NCH else (b + 1, 0)
                            if nb < B:
                                nt0 = nch * TC
                                pend_ln[0] = emit_ln(
                                    resid[:, :, nb, nt0:nt0 + TC], xh, xh,
                                    mk(f"xcast{nb}"), mk("xh_free"))
                                pend_ln[1] = (nb, nch)
                        for s in range(DS):
                            ti = tile[0]; tile[0] += 1
                            da = ti % 4
                            P.op("scalar", lambda e, i=i, s=s, da=da: e.activation(
                                dA[da], dbc_f, AF.Exp, scale=A_sb[:, i, s:s + 1]),
                                 waits=[dbc_done, mk("A"), dA_free[da]], inc=cs)
                            dA_rdy = (cs, cs.n)
                            wslot = (i * DS + s) % 2
                            P.op("vector", lambda e, i=i, s=s, wslot=wslot: e.tensor_copy(
                                wmul[wslot][:, 0:1], hcarry[:, i * DS + s:i * DS + s + 1]),
                                 waits=[carry0])
                            P.op("vector", lambda e, i=i, s=s, wslot=wslot: e.tensor_tensor(
                                wmul[wslot][:, 1:1 + TS], dbbc[:, s, :], xc[:, i, :], ALU.mult),
                                 waits=[bcast_done, xc_marks[i]])
                            P.op("vector", lambda e, i=i, s=s, da=da, wslot=wslot: e.tensor_tensor_scan(
                                hreg[:, s, :], dA[da], wmul[wslot],
                                0.0, ALU.mult, ALU.add),
                                 waits=[dA_rdy], inc=cv)
                            dA_free[da] = (cv, cv.n)
                            hs = ti % 2
                            P.op("vector", lambda e, s=s, hs=hs: e.tensor_tensor(
                                hh[hs], hreg[:, s, 1:1 + TS], cbc[:, s, :], ALU.mult),
                                 waits=[hh_free[hs]], inc=cv)
                            hh_rdy = (cv, cv.n)
                            P.op("tensor", lambda e, hs=hs, s=s: e.matmul(
                                PS[3][:, :], i128_sb[:], hh[hs],
                                start=(s == 0), stop=(s == DS - 1)),
                                 waits=([hh_rdy, ps_free[3]] if s == 0 else [hh_rdy]),
                                 inc=ct)
                            hh_free[hs] = (ct, ct.n)
                        yacc = (ct, ct.n)
                        P.op("vector", lambda e, i=i: e.tensor_copy(
                            hcarry[:, i * DS:(i + 1) * DS],
                            hreg[:, :, TS:TS + 1].rearrange("p s o -> p (s o)")), inc=cv)
                        hc_mark[i] = (cv, cv.n)
                        P.op("vector", lambda e, i=i: e.tensor_tensor(
                            sz[:, i, :], PS[3][:, :], sz[:, i, :], ALU.mult),
                             waits=[yacc, z_marks[i]], inc=cv)
                        ps_free[3] = (cv, cv.n)
                    mset("bcast_consumed", cv)
                    mset("dbc_free", cv)
                    ymix_all = (cv, cv.n)
                    mset("ymix_prev", cv)
                    if a_stop == "y":
                        P.op("vector", lambda e, b=b, t0=t0: e.tensor_copy(
                            resid[:, :, b, t0:t0 + TC], sz[:, 0:ND, :]),
                             waits=[ymix_all], inc=cv)
                        continue
                    for ec in range(ND):
                        pb = ec % 2
                        slot, rdy = load_slab(woutb_d, 2 * ec, NI // 2, cnt=2)
                        for kk in range(NI):
                            hhf, kc = kk // (NI // 2), kk % (NI // 2)
                            P.op("tensor", lambda e, slot=slot, kc=kc, kk=kk, pb=pb, hhf=hhf: e.matmul(
                                PS[pb][:, :], slab_stat(slot, NI // 2, kc, hhf), sz[:, kk, :],
                                start=(kk == 0), stop=False),
                                 waits=([rdy, ymix_all, ps_free[pb]] if kk == 0 else []),
                                 inc=(ct if kk == NI - 1 else None))
                        slab_done(slot)
                        P.op("tensor", lambda e, ec=ec, b=b, t0=t0, pb=pb: e.matmul(
                            PS[pb][:, :], i128_sb[:], resid[:, ec, b, t0:t0 + TC],
                            start=False, stop=True), inc=ct)
                        mmd = (ct, ct.n)
                        P.op("scalar", lambda e, ec=ec, b=b, t0=t0, pb=pb: e.copy(
                            resid[:, ec, b, t0:t0 + TC], PS[pb][:, :]),
                             waits=[mmd], inc=cs)
                        ps_free[pb] = (cs, cs.n)
                mset(f"res1_{b}", cs)

        def ffn(pin, pout, final):
            AB.reset(); AFr.reset()
            sqb = AB.view(ND, TS)
            xhb = AB.view(ND, 1 + S)
            xr = [AB.view(ND, TS) for _ in range(2)]
            xk = [AB.view(ND, TS) for _ in range(2)]
            rbuf = AB.view(ND, TS)
            kact = AB.view(ND, TS)
            LNBC[0] = (kact[:, 0, :], kact[:, 1, :])
            LNBC[1] = None
            ostg = AFr.view(ND, TS) if final else None
            fam = {"r": wrb_d, "k": wkb_d, "v": wvb_d}
            for b in range(B):
                prev = mk(f"{pin}_{b}")
                P.op("vector", lambda e: e.memset(
                    xhb[:, :, 0:1].rearrange("p a o -> p (a o)"), 0.0),
                     waits=[prev, mk("xh_free")], inc=cv)
                zc = (cv, cv.n)
                xh_done = None
                for slc in range(NSL):
                    t0 = slc * TS
                    xh_done = emit_ln(resid[:, :, b, t0:t0 + TS],
                                      xhb[:, :, 1 + t0:1 + t0 + TS], sqb, prev,
                                      mk("xh_free") if slc == 0 else None)
                xh_all = xh_done
                def emit_mix(slc):
                    t0 = slc * TS
                    xsl = slc % 2
                    xrs, xks = xr[xsl], xk[xsl]
                    cur = xhb[:, :, 1 + t0:1 + t0 + TS]
                    prv = xhb[:, :, t0:t0 + TS]
                    P.op("vector", lambda e, cur=cur, prv=prv, xrs=xrs: e.tensor_sub(xrs, cur, prv),
                         waits=[xh_all, zc, mk(f"xrk_free{xsl}")])
                    for nd in range(ND):
                        P.op("vector", lambda e, nd=nd, prv=prv, xrs=xrs, xks=xks: e.scalar_tensor_tensor(
                            xks[:, nd, :], xrs[:, nd, :], muk_sb[:, nd:nd + 1], prv[:, nd, :],
                            ALU.mult, ALU.add))
                    for nd in range(ND):
                        P.op("vector", lambda e, nd=nd, prv=prv, xrs=xrs: e.scalar_tensor_tensor(
                            xrs[:, nd, :], xrs[:, nd, :], mur_sb[:, nd:nd + 1], prv[:, nd, :],
                            ALU.mult, ALU.add), inc=(cv if nd == ND - 1 else None))
                    return (cv, cv.n)
                mix_marks = {0: emit_mix(0)}
                for slc in range(NSL):
                    t0 = slc * TS
                    xsl = slc % 2
                    xrs, xks = xr[xsl], xk[xsl]
                    mix_done = mix_marks[slc]
                    for mat in ("r", "k"):
                        src = xrs if mat == "r" else xks
                        for mc in range(ND):
                            if mc % 2 == 0:
                                slot, rdy = load_slab(fam[mat], mc, ND, cnt=2)
                            sub = mc % 2
                            pb = mc % 2
                            for kc in range(ND):
                                P.op("tensor", lambda e, slot=slot, kc=kc, src=src, pb=pb, sub=sub: e.matmul(
                                    PS[pb][:, :], slab_stat(slot, ND, kc, sub), src[:, kc, :],
                                    start=(kc == 0), stop=(kc == ND - 1)),
                                     waits=([rdy, mix_done, ps_free[pb]] if kc == 0 else []),
                                     inc=(ct if kc == ND - 1 else None))
                            if mc % 2 == 1:
                                slab_done(slot)
                            mmd = (ct, ct.n)
                            if mat == "r":
                                P.op("scalar", lambda e, mc=mc, pb=pb: e.activation(
                                    rbuf[:, mc, :], PS[pb][:, :], AF.Sigmoid),
                                     waits=[mmd, mk("rbuf_free")], inc=cs)
                                ps_free[pb] = (cs, cs.n)
                            else:
                                P.op("scalar", lambda e, mc=mc, pb=pb: e.activation(
                                    kact[:, mc, :], PS[pb][:, :], AF.Relu),
                                     waits=[mmd], inc=cs)
                                ps_free[pb] = (cs, cs.n)
                                P.op("vector", lambda e, mc=mc: e.tensor_tensor(
                                    kact[:, mc, :], kact[:, mc, :], kact[:, mc, :], ALU.mult),
                                     waits=[(cs, cs.n)], inc=cv)
                    mset(f"xrk_free{xsl}", ct)
                    k_done = (cv, cv.n)
                    if slc + 1 < NSL:
                        mix_marks[slc + 1] = emit_mix(slc + 1)
                    r_done = (cs, cs.n)
                    for mc in range(ND):
                        if mc % 2 == 0:
                            slot, rdy = load_slab(fam["v"], mc, ND, cnt=2)
                        sub = mc % 2
                        pb = mc % 2
                        for kc in range(ND):
                            P.op("tensor", lambda e, slot=slot, kc=kc, pb=pb, sub=sub: e.matmul(
                                PS[pb][:, :], slab_stat(slot, ND, kc, sub), kact[:, kc, :],
                                start=(kc == 0), stop=(kc == ND - 1)),
                                 waits=([rdy, k_done, ps_free[pb]] if kc == 0 else []),
                                 inc=(ct if kc == ND - 1 else None))
                        if mc % 2 == 1:
                            slab_done(slot)
                        mmd = (ct, ct.n)
                        P.op("vector", lambda e, mc=mc, pb=pb: e.tensor_tensor(
                            rbuf[:, mc, :], PS[pb][:, :], rbuf[:, mc, :], ALU.mult),
                             waits=[mmd, r_done], inc=cv)
                        ps_free[pb] = (cv, cv.n)
                        if final:
                            P.op("vector", lambda e, mc=mc, b=b, t0=t0: e.tensor_tensor(
                                ostg[:, mc, :], resid[:, mc, b, t0:t0 + TS],
                                rbuf[:, mc, :], ALU.add),
                                 waits=[(cd_out, cd_out.n)], inc=cv)
                        else:
                            P.op("vector", lambda e, mc=mc, b=b, t0=t0: e.tensor_tensor(
                                resid[:, mc, b, t0:t0 + TS],
                                resid[:, mc, b, t0:t0 + TS], rbuf[:, mc, :], ALU.add),
                                 inc=cv)
                    mset("rbuf_free", cv)
                    if final:
                        dma(out_d[b, :, t0:t0 + TS].rearrange("(a p) t -> p a t", p=128),
                            ostg[:, :, :], cd_out, waits=[(cv, cv.n)])
                mset("xh_free", ct)
                mset(f"{pout}_{b}", cv)

        def attn_b():
            AB.reset(); AFr.reset()
            sqb = AB.view(ND, TS)
            xhc = AB.view(ND, HALF)
            ob = AB.view(NPGA, HALF)
            LNBC[0] = (ob[:, 0, 0:TS], ob[:, 1, 0:TS])
            LNBC[1] = None
            cvb = AB.view(HALF)
            vdb = AB.view(HALF)
            bcx = [AB.view(HALF) for _ in range(2)]
            bcd = [AB.view(HALF) for _ in range(2)]
            bcp = [AB.view(HALF) for _ in range(2)]
            bcr = [AB.view(HALF) for _ in range(2)]
            p1b = AB.view(S)[0:H, :]
            p2eb = AB.view(S)[0:H, :]
            ceb = AB.view(HALF)[0:H, :]
            deb = AB.view(HALF)[0:H, :]
            eeb = AB.view(HALF)[0:H, :]
            pexb = AB.view(HALF)[0:H, :]
            rdenb = AB.view(HALF)[0:H, :]
            nbc = AFr.view(S)
            cdeF = nbc[0:80, 0:HALF]
            cums = AFr.view(HALF)
            numt = cums
            dt1 = AFr.view(HALF)[0:H, :]
            dt2 = dt1

            for q in range(NSL):
                dma(lrow[0][:], nrow_d[:, q * TS:(q + 1) * TS], cd_w,
                    waits=[mk("lrow_free"), (ct, ct.n), (cv, cv.n), (cs, cs.n)])
                P.op("tensor", lambda e, q=q: e.matmul(
                    PSb[:, :], ones1_sb[:], lrow[0][:],
                    start=True, stop=True),
                     waits=[mk("psb_free"), cd_w], inc=ct)
                mset("lrow_free", ct)
                P.op("vector", lambda e, q=q: e.tensor_copy(
                    nbc[:, q * TS:(q + 1) * TS], PSb[:, :]),
                     waits=[(ct, ct.n)], inc=cv)
                mset("psb_free", cv)
            nbc_done = (cv, cv.n)
            for which, aa, bb in (("p1", a1_sb, b1_sb), ("p2", a2_sb, b2_sb)):
                for q in range(NSL):
                    for pg in range(NPG):
                        P.op("scalar", lambda e, aa=aa, bb=bb, pg=pg, q=q: e.activation(
                            sqb[:, 0, :], nbc[:, q * TS:(q + 1) * TS],
                            AF.Sin, bias=bb[:, pg:pg + 1], scale=aa[:, pg:pg + 1]),
                             waits=[nbc_done, (ct, ct.n)], inc=cs)
                        P.op("tensor", lambda e, pg=pg: e.matmul(
                            PSm[0:H, :], cwsel_sb[:, pg, :], sqb[:, 0, :],
                            start=(pg == 0), stop=(pg == NPG - 1)),
                             waits=[(cs, cs.n)] + ([mk("psm_free")] if pg == 0 else []),
                             inc=ct)
                    P.op("vector", lambda e, which=which, q=q: e.tensor_copy(
                        (p1b if which == "p1" else p2eb)[:, q * TS:(q + 1) * TS],
                        PSm[0:H, :]),
                         waits=[(ct, ct.n)], inc=cv)
                    mset("psm_free", cv)
            p12 = (cv, cv.n)
            mset("sq_free", ct)
            P.op("vector", lambda e: e.tensor_scalar(
                p2eb, p2eb, -20.0, 20.0, ALU.max, ALU.min), waits=[p12], inc=cv)
            P.op("scalar", lambda e: e.activation(p2eb, p2eb, AF.Exp),
                 waits=[(cv, cv.n)], inc=cs)
            p2e_done = (cs, cs.n)

            for b in range(B):
                prev = mk(f"res3_{b}")
                P.op("vector", lambda e: e.memset(ncarry[:], 0.0), waits=[prev], inc=cv)
                P.op("vector", lambda e: e.memset(dcarry[0:H, :], 0.0), inc=cv)
                carry0 = (cv, cv.n)
                for hf in range(2):
                    h0 = hf * HALF
                    xh_done = None
                    for slc in range(NHS):
                        t0 = h0 + slc * TS
                        xh_done = emit_ln(resid[:, :, b, t0:t0 + TS],
                                          xhc[:, :, slc * TS:(slc + 1) * TS], sqb, prev,
                                          mk("xh_free") if slc == 0 else None)
                    xh_all = xh_done
                    c_stop = cfg.get("c_stop")
                    if c_stop == "xh":
                        P.op("vector", lambda e, b=b, h0=h0: e.tensor_copy(
                            resid[:, :, b, h0:h0 + HALF], xhc), waits=[xh_all], inc=cv)
                        continue
                    for slc in range(NHS):
                        P.op("tensor", lambda e, slc=slc, h0=h0: e.matmul(
                            PSm[0:80, :], i16_sb[:],
                            p1b[:, h0 + slc * TS:h0 + (slc + 1) * TS],
                            start=True, stop=False),
                             waits=[p12, mk("psm_free")])
                        for kc in range(ND):
                            P.op("tensor", lambda e, kc=kc, slc=slc: e.matmul(
                                PSm[0:80, :], k123_sb[:, kc, :],
                                xhc[:, kc, slc * TS:(slc + 1) * TS],
                                start=False, stop=(kc == ND - 1)),
                                 waits=([xh_all] if kc == 0 else []),
                                 inc=(ct if kc == ND - 1 else None))
                        kmm = (ct, ct.n)
                        P.op("vector", lambda e, slc=slc: e.tensor_scalar(
                            cdeF[:, slc * TS:(slc + 1) * TS], PSm[0:80, :],
                            -20.0, 20.0, ALU.max, ALU.min),
                             waits=[kmm, mk("cde_free")], inc=cv)
                        mset("psm_free", cv)
                    P.op("scalar", lambda e: e.activation(
                        cdeF[:, :], cdeF[:, :], AF.Exp), waits=[(cv, cv.n)], inc=cs)
                    cde_done = (cs, cs.n)
                    P.op("vector", lambda e: e.tensor_copy(ceb, cdeF[0:H, :]),
                         waits=[cde_done, mk("rowsb_free")])
                    P.op("vector", lambda e: e.tensor_copy(deb, cdeF[32:32 + H, :]))
                    P.op("vector", lambda e: e.tensor_copy(eeb, cdeF[64:64 + H, :]), inc=cv)
                    rows_b16 = (cv, cv.n)
                    P.op("vector", lambda e, h0=h0: e.tensor_tensor(
                        pexb, p2eb[:, h0:h0 + HALF], eeb, ALU.mult),
                         waits=[p2e_done], inc=cv)
                    pex_done = (cv, cv.n)
                    P.op("vector", lambda e: e.tensor_tensor_scan(
                        dt1, onesbig_sb[0:H, 0:HALF], ceb, 0.0,
                        ALU.mult, ALU.add),
                         waits=[carry0])
                    P.op("vector", lambda e: e.tensor_scalar(
                        dt1, dt1, dcarry[0:H, :], None, ALU.add), inc=cv)
                    P.op("vector", lambda e: e.tensor_copy(
                        dcarry[0:H, :], dt1[:, HALF - 1:HALF]), inc=cv)
                    P.op("vector", lambda e: e.tensor_tensor(dt2, dt1, pexb, ALU.mult))
                    P.op("vector", lambda e: e.tensor_tensor(dt2, dt2, deb, ALU.add))
                    def _recip(e):
                        with nc.allow_low_precision("bf16 output rounding only"):
                            return e.reciprocal(rdenb, dt2)
                    P.op("vector", _recip, inc=cv)
                    rows_done = (cv, cv.n)
                    if c_stop == "rows":
                        def _dump_rows(e, b=b, h0=h0):
                            return e.tensor_copy(resid[0:H, 0, b, h0:h0 + HALF], ceb)
                        P.op("vector", _dump_rows, waits=[rows_done])
                        P.op("vector", lambda e, b=b, h0=h0: e.tensor_copy(
                            resid[0:H, 1, b, h0:h0 + HALF], deb))
                        P.op("vector", lambda e, b=b, h0=h0: e.tensor_copy(
                            resid[0:H, 2, b, h0:h0 + HALF], pexb))
                        P.op("vector", lambda e, b=b, h0=h0: e.tensor_copy(
                            resid[0:H, 3, b, h0:h0 + HALF], rdenb), inc=cv)
                        if ND > 4:
                            pass
                        P.op("vector", lambda e, b=b, h0=h0: e.tensor_copy(
                            resid[32:32 + H, 0, b, h0:h0 + HALF], dt1), inc=cv)
                        P.op("vector", lambda e, b=b, h0=h0: e.tensor_copy(
                            resid[32:32 + H, 1, b, h0:h0 + HALF], dt2), inc=cv)
                        P.op("vector", lambda e, b=b, h0=h0: e.tensor_copy(
                            resid[32:32 + H, 2, b, h0:h0 + HALF], p2eb[:, h0:h0 + HALF]), inc=cv)
                        P.op("vector", lambda e, b=b, h0=h0: e.tensor_copy(
                            resid[32:32 + H, 3, b, h0:h0 + HALF], eeb), inc=cv)
                        mset("rowsb_free", ct)
                        continue
                    rot = [PSb, PS[4], PS[5]]
                    rot_free = [mk("psb_free"), mk("bc4_free"), mk("bc5_free")]
                    ri = [0]
                    bc_marks = {}

                    def emit_bcasts(pg):
                        sl = pg % 2
                        for src_rows, dst in ((ceb, bcx), (deb, bcd), (pexb, bcp), (rdenb, bcr)):
                            for slc in range(NHS):
                                k = ri[0] % 3; ri[0] += 1
                                bank = rot[k]
                                P.op("tensor", lambda e, pg=pg, src_rows=src_rows, slc=slc, bank=bank: e.matmul(
                                    bank[:, :], bcsel_sb[:, pg, :],
                                    src_rows[:, slc * TS:(slc + 1) * TS],
                                    start=True, stop=True),
                                     waits=[rows_done, rows_b16, rot_free[k],
                                            bc_marks.get((sl, "free"))], inc=ct)
                                P.op("vector", lambda e, dst=dst, slc=slc, bank=bank, sl=sl: e.tensor_copy(
                                    dst[sl].rearrange("p (a t) -> p a t", a=NHS)[:, slc, :],
                                    bank[:, :]), waits=[(ct, ct.n)], inc=cv)
                                rot_free[k] = (cv, cv.n)
                        bc_marks[(sl, "ready")] = (cv, cv.n)

                    emit_bcasts(0)
                    for pg in range(NPGA):
                        sl = pg % 2
                        bcasts = bc_marks[(sl, "ready")]
                        slot, rdy = load_slab(vwb_d, pg, ND)
                        for slc in range(NHS):
                            pb = slc % 2
                            for kc in range(ND):
                                P.op("tensor", lambda e, slot=slot, kc=kc, slc=slc, pb=pb: e.matmul(
                                    PS[pb][:, :], slab_stat(slot, ND, kc),
                                    xhc[:, kc, slc * TS:(slc + 1) * TS],
                                    start=(kc == 0), stop=(kc == ND - 1)),
                                     waits=([rdy, xh_all, ps_free[pb]] if kc == 0 else []),
                                     inc=(ct if kc == ND - 1 else None))
                            if slc == NHS - 1:
                                slab_done(slot)
                            vmm = (ct, ct.n)
                            P.op("vector", lambda e, slc=slc, pb=pb, sl=sl: e.tensor_tensor(
                                cvb.rearrange("p (a t) -> p a t", a=NHS)[:, slc, :],
                                PS[pb][:, :],
                                bcx[sl].rearrange("p (a t) -> p a t", a=NHS)[:, slc, :], ALU.mult),
                                 waits=[vmm, bcasts])
                            P.op("vector", lambda e, slc=slc, pb=pb, sl=sl: e.tensor_tensor(
                                vdb.rearrange("p (a t) -> p a t", a=NHS)[:, slc, :],
                                PS[pb][:, :],
                                bcd[sl].rearrange("p (a t) -> p a t", a=NHS)[:, slc, :], ALU.mult),
                                 inc=cv)
                            ps_free[pb] = (cv, cv.n)
                        evd = (cv, cv.n)
                        if pg + 1 < NPGA:
                            emit_bcasts(pg + 1)
                        P.op("vector", lambda e, pg=pg: e.tensor_tensor_scan(
                            cums, onesbig_sb[:, 0:HALF], cvb, 0.0,
                            ALU.mult, ALU.add), waits=[evd, carry0])
                        P.op("vector", lambda e, pg=pg: e.tensor_scalar(
                            cums, cums, ncarry[:, pg:pg + 1], None, ALU.add), inc=cv)
                        P.op("vector", lambda e, pg=pg: e.tensor_copy(
                            ncarry[:, pg:pg + 1], cums[:, HALF - 1:HALF]), inc=cv)
                        P.op("vector", lambda e, sl=sl: e.tensor_tensor(numt, cums, bcp[sl], ALU.mult))
                        P.op("vector", lambda e: e.tensor_tensor(numt, numt, vdb, ALU.add))
                        P.op("vector", lambda e, pg=pg, sl=sl: e.tensor_tensor(
                            ob[:, pg, :], numt, bcr[sl], ALU.mult),
                             waits=[mk("ob_free")], inc=cv)
                        bc_marks[(sl, "free")] = (cv, cv.n)
                    mset("psb_free", cv)
                    mset("bc4_free", cv)
                    mset("bc5_free", cv)
                    ob_done = (cv, cv.n)
                    if c_stop == "ob":
                        for pgd in range(min(NPGA, ND)):
                            P.op("vector", lambda e, b=b, h0=h0, pgd=pgd: e.tensor_copy(
                                resid[:, pgd, b, h0:h0 + HALF], ob[:, pgd, :]),
                                 waits=[ob_done], inc=(cv if pgd == min(NPGA, ND) - 1 else None))
                        mset("ob_free", cv)
                        continue
                    for wc in range(ND):
                        slot, rdy = load_slab(owb_d, wc, NPGA)
                        for slc in range(NHS):
                            pb = slc % 2
                            for pg in range(NPGA):
                                P.op("tensor", lambda e, slot=slot, pg=pg, slc=slc, pb=pb: e.matmul(
                                    PS[pb][:, :], slab_stat(slot, NPGA, pg),
                                    ob[:, pg, slc * TS:(slc + 1) * TS],
                                    start=(pg == 0), stop=False),
                                     waits=([rdy, ob_done, ps_free[pb]] if pg == 0 else []))
                            P.op("tensor", lambda e, wc=wc, b=b, h0=h0, slc=slc, pb=pb: e.matmul(
                                PS[pb][:, :], i128_sb[:],
                                resid[:, wc, b, h0 + slc * TS:h0 + (slc + 1) * TS],
                                start=False, stop=True), inc=ct)
                            mmd = (ct, ct.n)
                            P.op("scalar", lambda e, wc=wc, b=b, h0=h0, slc=slc, pb=pb: e.copy(
                                resid[:, wc, b, h0 + slc * TS:h0 + (slc + 1) * TS],
                                PS[pb][:, :]), waits=[mmd], inc=cs)
                            ps_free[pb] = (cs, cs.n)
                        slab_done(slot)
                    mset("ob_free", ct)
                    mset("cde_free", cs)
                    mset("rowsb_free", ct)
                    mset("xh_free", ct)
                mset(f"res4_{b}", cs)

        phases = cfg.get("phases", "ABCD")
        dump = AFr
        def dump_resid():
            ost = arF[:, 0:TS * ND].rearrange("p (a t) -> p a t", a=ND, t=TS)
            for b in range(B):
                for slc in range(NSL):
                    t0 = slc * TS
                    P.op("vector", lambda e, b=b, t0=t0: e.tensor_copy(
                        ost, resid[:, :, b, t0:t0 + TS]),
                         waits=[(cd_out, cd_out.n), (ct, ct.n), (cs, cs.n)], inc=cv)
                    dma(out_d[b, :, t0:t0 + TS].rearrange("(a p) t -> p a t", p=128),
                        ost, cd_out, waits=[(cv, cv.n)])

        if "A" in phases:
            attn_a()
            barrier()
        else:
            for b in range(B):
                mset(f"res1_{b}", cv)
        for b in range(B):
            M[f"res2_{b}"] = M[f"res1_{b}"]
        if "B" in phases:
            ffn("res1", "res2b", final=False)
            barrier()
        else:
            for b in range(B):
                M[f"res2b_{b}"] = M[f"res1_{b}"]
        for b in range(B):
            M[f"res3_{b}"] = M[f"res2b_{b}"]
        if "C" in phases:
            attn_b()
            barrier()
        else:
            for b in range(B):
                M[f"res4_{b}"] = M[f"res3_{b}"]
        if "D" in phases:
            ffn("res4", "res5", final=True)
        else:
            dump_resid()

        P.op("gpsimd", lambda e: e.nop(), waits=[(cd_out, cd_out.n)])
        P.op("sync", lambda e: e.nop(), waits=[(cd_out, cd_out.n)])

        block = stack.enter_context(nc.Block())
        P.emit(block)
    return nc


def kernel(**inputs):
    cfg = default_cfg()
    n_cores = 8
    B = cfg["B"]
    nc = build_nc(cfg)
    in_maps = []
    for core in range(n_cores):
        shard = dict(inputs)
        shard["L0_attn_a_x"] = inputs["L0_attn_a_x"][core * B:(core + 1) * B]
        in_maps.append(host_inputs(cfg, shard))
    res = run_bass_kernel_spmd(nc, in_maps, core_ids=list(range(n_cores)))
    outs = []
    for core in range(n_cores):
        o = np.asarray(res.results[core]["out"])
        outs.append(o.transpose(0, 2, 1))
    return np.ascontiguousarray(np.concatenate(outs, axis=0)).astype(np.float32)
